# version 01 (final)
# baseline (speedup 1.0000x reference)
import numpy as np
import ml_dtypes
import concourse.bass as bass
import concourse.mybir as mybir
from concourse.bass_utils import run_bass_kernel_spmd

F32 = mybir.dt.float32
BF16 = mybir.dt.bfloat16
I32 = mybir.dt.int32
AF = mybir.ActivationFunctionType
ALU = mybir.AluOpType
BF = ml_dtypes.bfloat16

NCORE = 8
T = 1024
HT = 512
HID = 4096
INTER = 11008
NT = 86
KJ = 32
ND = 32
KD = 86
L = 4096
EPS = 1e-5
RN = 1.0 / HID


def host_consts():
    c = {}
    c["c_ident"] = np.eye(128, dtype=np.float32)
    sl = np.zeros((128, 171), np.float32)
    sl[:, 85] = 1.0
    c["c_slide"] = sl.astype(BF)
    c["c_shift"] = (4 * (np.arange(128) % 8)).astype(np.int32).reshape(128, 1)
    l = np.arange(L, dtype=np.float64)
    Om = np.exp(-2j * np.pi * l / L)
    w = (1.0 - Om) / (1.0 + Om)
    cc = 2.0 / (1.0 + Om)
    c["c_wtab"] = np.stack([w.real, w.imag]).astype(np.float32)
    c["c_ctabr"] = cc.real.astype(np.float32).reshape(128, 32)
    c["c_ctabi"] = cc.imag.astype(np.float32).reshape(128, 32)
    j1 = np.arange(64, dtype=np.float64)
    E1 = np.exp(2j * np.pi * np.outer(j1, j1) / 64.0)
    TW = np.exp(2j * np.pi * np.outer(j1, j1) / 4096.0)
    E2 = np.exp(2j * np.pi * np.outer(j1, j1) / 64.0) / L
    c["c_e1r"] = E1.real.astype(np.float32)
    c["c_e1i"] = E1.imag.astype(np.float32)
    c["c_e1in"] = (-E1.imag).astype(np.float32)
    c["c_twr"] = TW.real.astype(np.float32)
    c["c_twi"] = TW.imag.astype(np.float32)
    c["c_e2rs"] = E2.real.astype(np.float32)
    c["c_e2ins"] = (-E2.imag).astype(np.float32)
    return c


class Prog:
    def __init__(self, debug_taps=()):
        self.debug_taps = set(debug_taps)
        self.nc = bass.Bass("TRN2", target_bir_lowering=False)
        self.sems = {}
        self.cnt = {}
        self.marks = {}

    def sem(self, name):
        if name not in self.sems:
            self.sems[name] = self.nc.alloc_semaphore(name=name)
            self.cnt[name] = 0
        return self.sems[name]

    def dinc(self, name, instr):
        instr.then_inc(self.sem(name), 16)
        self.cnt[name] += 16
        return self.cnt[name]

    def cinc(self, name, instr):
        instr.then_inc(self.sem(name), 1)
        self.cnt[name] += 1
        return self.cnt[name]

    def val(self, name):
        self.sem(name)
        return self.cnt[name]

    @staticmethod
    def w(eng, sem, v):
        if v > 0:
            eng.wait_ge(sem, v)


def build_program(debug_taps=(), phases=("p0", "p1", "p2", "p3", "p4", "p5", "p6", "p7")):
    P = Prog(debug_taps)
    nc = P.nc
    P.phases = set(phases)

    D = {}
    D["u"] = nc.dram_tensor("u", [T, HID], F32, kind="ExternalInput")
    for nm, shp, dt in [
        ("lam", [64, 2], F32), ("pri", [64, 2], F32), ("qri", [64, 2], F32),
        ("vcri", [64, 128], F32), ("ctri", [128], F32), ("bp", [64, 1], F32),
        ("dd", [1], F32), ("ls", [1], F32),
        ("gq", [512, INTER], I32), ("gs", [32, INTER], F32), ("gz", [32, 1376], I32),
        ("uq", [512, INTER], I32), ("us", [32, INTER], F32), ("uz", [32, 1376], I32),
        ("dq", [1376, HID], I32), ("ds", [86, HID], F32), ("dz", [86, 512], I32),
    ]:
        D[nm] = nc.dram_tensor(nm, shp, dt, kind="ExternalInput")
    for nm, arr in host_consts().items():
        dt = {np.dtype(np.float32): F32, np.dtype(BF): BF16, np.dtype(np.int32): I32}[arr.dtype]
        D[nm] = nc.dram_tensor(nm, list(arr.shape), dt, kind="ExternalInput")
    D["out"] = nc.dram_tensor("out", [T, HID], F32, kind="ExternalOutput")

    D["wgu_scr"] = nc.dram_tensor("wgu_scr", [NT, 128, 8192], BF16)
    D["wd_scr"] = nc.dram_tensor("wd_scr", [KD, 128, 4096], BF16)
    D["h_scr"] = nc.dram_tensor("h_scr", [KD, 128, T], BF16)
    for nm in ["x0t", "x1t", "x2t", "x3t"]:
        D[nm] = nc.dram_tensor(nm, [ND, 128, T], F32)
    D["xn1_scr"] = nc.dram_tensor("xn1_scr", [T, HID], BF16)
    D["sg_bf"] = nc.dram_tensor("sg_bf", [32, INTER], BF16)
    D["szg_scr"] = nc.dram_tensor("szg_scr", [32, INTER], BF16)
    D["szu_scr"] = nc.dram_tensor("szu_scr", [32, INTER], BF16)
    D["szd_scr"] = nc.dram_tensor("szd_scr", [86, HID], BF16)
    D["su_bf"] = nc.dram_tensor("su_bf", [32, INTER], BF16)
    D["sd_bf"] = nc.dram_tensor("sd_bf", [86, HID], BF16)
    D["k8_scr"] = nc.dram_tensor("k8_scr", [8, L], F32)
    D["a_scr"] = nc.dram_tensor("a_scr", [2, L], F32)
    D["kpad_scr"] = nc.dram_tensor("kpad_scr", [128 + L + 128], BF16)
    D["row_scr"] = nc.dram_tensor("row_scr", [4, T], F32)
    D["c2s_scr"] = nc.dram_tensor("c2s_scr", [1], F32)
    P.D = D

    S = {}
    a = nc.alloc_sbuf_tensor

    class TView:
        def __init__(self, back, dtype, Pn, Fn):
            esz = 2 if dtype in (BF16, mybir.dt.float16) else 4
            bsz = 2 if back.dtype in (BF16, mybir.dt.float16) else 4
            self.back, self.dtype, self.Pn, self.Fn = back, dtype, Pn, Fn
            self.pitch = int(back.shape[1]) * bsz // esz
            assert Fn <= self.pitch and Pn <= int(back.shape[0]), (Pn, Fn, self.pitch)
            self.shape = [Pn, Fn]

        def fullap(self):
            return self.back.ap().bitcast(self.dtype)

        def ap(self):
            return self.fullap()[0:self.Pn, 0:self.Fn]

        def __getitem__(self, key):
            return self.ap()[key]

    def vrt(name, shape, dtype, backing):
        bt = S[backing] if isinstance(backing, str) else backing
        return TView(bt, dtype, shape[0], shape[1])
    S["xnT"] = a("xnT", [128, KJ * T], BF16)
    S["chunk0"] = a("chunk0", [128, 8192], BF16)
    S["chunk1"] = a("chunk1", [128, 8192], BF16)
    S["chunk"] = [S["chunk0"], S["chunk1"]]
    S["kslide"] = a("kslide", [128, 4224], BF16)
    S["ident"] = a("ident", [128, 128], F32)
    S["slide"] = a("slide", [128, 171], BF16)
    S["shift"] = a("shift", [128, 1], I32)
    S["ctabr"] = a("ctabr", [128, 32], F32)
    S["ctabi"] = a("ctabi", [128, 32], F32)
    for nm in ["e1r", "e1i", "e1in", "twr", "twi", "e2rs", "e2ins"]:
        S[nm] = a(nm, [64, 64], F32)
    S["dcol"] = a("dcol", [128, 1], F32)
    S["qbuf"] = [a(f"qbuf{i}", [128, 512], I32) for i in range(6)]
    S["sbc"] = [a(f"sbc{i}", [128, 512], BF16) for i in range(6)]
    S["w4i"] = [a(f"w4i{i}", [128, 512], I32) for i in range(3)]
    S["w4d"] = S["w4i"]
    S["wds"] = [a(f"wds{i}", [128, 512], BF16) for i in range(2)]
    S["hg0_real"] = a("hg0", [128, T], F32)
    S["hg1_real"] = a("hg1", [128, T], F32)
    S["hg"] = [S["hg0_real"], S["hg1_real"]]
    S["hbf"] = [a(f"hbf{i}", [128, T], BF16) for i in range(2)]
    S["hstr"] = [a(f"hstr{i}", [128, T], BF16) for i in range(4)]
    S["wd4"] = [a(f"wd4{i}", [128, 512], BF16) for i in range(4)]
    S["r_bc"] = a("r_bc", [128, T], F32)
    S["mr_bc"] = a("mr_bc", [128, T], F32)
    S["xres"] = [a(f"xres{i}", [128, T], F32) for i in range(2)]
    S["fence"] = a("fence", [128, 64], F32)
    S["xout"] = [a(f"xout{i}", [128, T], F32) for i in range(2)]
    S["hgr"] = a("hgr", [86, T], BF16)
    S["hgl"] = a("hgl", [86, T], BF16)
    S["szt_g"] = [a(f"szt_g{i}", [32, 128], BF16) for i in range(2)]
    S["szt_u"] = [a(f"szt_u{i}", [32, 128], BF16) for i in range(2)]
    S["szdt"] = [a(f"szdt{i}", [86, 128], BF16) for i in range(4)]
    for nm in ["st_m", "st_q", "st_v", "st_r"]:
        S[nm] = a(nm, [1, T], F32)
    for nm in ["cA", "cB", "cC", "cE"]:
        S[nm] = a(nm, [128, 1], F32)
    S["xnst"] = None
    S["x0st"] = [a(f"x0st{i}", [128, 128], F32) for i in range(2)]
    S["lamT"] = a("lamT", [64, 2], F32)
    S["priT"] = a("priT", [64, 2], F32)
    S["qriT"] = a("qriT", [64, 2], F32)
    S["ctT"] = a("ctT", [64, 2], F32)
    S["bpT"] = a("bpT", [64, 1], F32)
    S["bcol"] = a("bcol", [64, 2], F32)
    S["vst"] = a("vst", [128, 8], F32)
    S["vt"] = [a(f"vt{i}", [64, 1], F32) for i in range(4)]
    S["lsr"] = a("lsr", [1, 1], F32)
    S["expt"] = a("expt", [1, 1], F32)
    S["c2s"] = a("c2s", [1, 1], F32)
    S["c2scol"] = a("c2scol", [64, 1], F32)
    S["zrow"] = a("zrow", [1, 256], BF16)
    S["k2bf"] = a("k2bf", [64, 64], BF16)

    S["xtile"] = [vrt("xtile0", [128, HID], F32, "chunk0"),
                  vrt("xtile1", [128, HID], F32, "chunk1")]
    S["xnst_lo"] = vrt("xnst_lo", [128, 2048], BF16, "r_bc")
    S["xnst_hi"] = vrt("xnst_hi", [128, 2048], BF16, "mr_bc")
    S["ot"] = [vrt("ot0", [128, 128], F32, S["x0st"][0]),
               vrt("ot1", [128, 128], F32, S["x0st"][1])]
    S["sch_f"] = vrt("sch_f", [86, 1376], F32, "chunk0")
    S["zch_u"] = vrt("zch_u", [86, 1376], I32, "chunk1")
    S["zch_f"] = vrt("zch_f", [86, 1376], F32, "xnT")
    S["sch_b"] = vrt("sch_b", [86, 1376], BF16, "kslide")
    S["zch_i"] = vrt("zch_i", [86, 172], I32, S["hstr"][0])
    S["szch"] = vrt("szch", [86, 1376], BF16, "mr_bc")
    S["wbc"] = [vrt("wbc0", [64, 512], F32, S["hbf"][0]),
                vrt("wbc1", [64, 512], F32, S["hbf"][1])]
    S["dnr"] = vrt("dnr", [64, 512], F32, S["qbuf"][0])
    S["dni"] = vrt("dni", [64, 512], F32, S["qbuf"][1])
    S["dmag"] = vrt("dmag", [64, 512], F32, S["qbuf"][2])
    S["drec"] = vrt("drec", [64, 512], F32, S["qbuf"][3])
    S["rri"] = vrt("rri", [128, 512], F32, "hg0_real")
    S["k8"] = vrt("k8", [8, 512], F32, "hg1_real")
    ktback = [S["w4i"][0], S["w4i"][1], S["w4i"][2], S["wds"][0],
              S["x0st"][0], S["x0st"][1], S["hstr"][0], S["hstr"][1]]
    S["kt"] = [vrt(f"kt{i}", [128, 32], F32, ktback[i]) for i in range(8)]
    atback = [S["hstr"][2], S["hstr"][3], S["wd4"][1], S["wd4"][2],
              S["wd4"][3], S["wds"][1]]
    S["at"] = [vrt(f"at{i}", [128, 32], F32, atback[i]) for i in range(6)]
    S["a1r"] = vrt("a1r", [64, 64], F32, "hgr")
    S["a1i"] = vrt("a1i", [64, 64], F32, "hgl")
    S["s1r"] = vrt("s1r", [64, 64], F32, S["xout"][0])
    S["s1i"] = vrt("s1i", [64, 64], F32, S["xout"][1])
    S["t1r"] = vrt("t1r", [64, 64], F32, S["xres"][0])
    S["t1i"] = vrt("t1i", [64, 64], F32, S["xres"][1])
    S["vc"] = a("vc", [64, 128], F32)
    S["vcr"] = vrt("vcr", [64, 64], F32, S["szdt"][0])
    S["vci"] = vrt("vci", [64, 64], F32, S["szdt"][1])
    S["vcrT"] = vrt("vcrT", [64, 64], F32, S["szdt"][2])
    S["vciT"] = vrt("vciT", [64, 64], F32, S["szdt"][3])
    S["xres4"] = [S["xres"][0], S["xres"][1],
                  vrt("xres2", [128, T], F32, "hg0_real"),
                  vrt("xres3", [128, T], F32, "hg1_real")]
    S["lntmp"] = vrt("lntmp", [128, T], F32, "chunk0")
    S["xbf"] = vrt("xbf", [128, T], BF16, S["hbf"][0])
    S["xsq"] = vrt("xsq", [128, T], BF16, S["hbf"][1])
    S["xg"] = vrt("xg", [32, T], BF16, "hgr")
    S["xgl"] = vrt("xgl", [32, T], BF16, "hgl")
    P.S = S

    def _fence(eng, sem_name):
        P.dinc(sem_name, eng.dma_start(S["fence"].ap(), D["c_ident"].ap()[0:128, 0:64]))
    P.fence = _fence

    if "p0" in P.phases or "p0c" in P.phases:
        emit_p0_consts(P)
    if "p0" in P.phases or "p0s" in P.phases:
        emit_scales(P)
    if "p0" in P.phases or "p04" in P.phases:
        emit_s4(P, 4)
    elif "p04a" in P.phases:
        emit_s4(P, 1)
    elif "p04b" in P.phases:
        emit_s4(P, 2)
    elif "p04c" in P.phases:
        emit_s4(P, 3)
    if "p0" in P.phases or "p0l" in P.phases:
        emit_ln1(P)
    if "p1" in P.phases:
        emit_ffn_gateup(P, 1)
    if "p2" in P.phases:
        emit_ffn_down(P, 1)
    elif "p2h" in P.phases:
        emit_ffn_down(P, 1, upto=1)
    elif "p2z" in P.phases:
        emit_ffn_down(P, 1, upto=3)
    elif "p2n" in P.phases:
        emit_ffn_down(P, 1, upto=4)
    elif "p2d" in P.phases:
        emit_ffn_down(P, 1, upto=5)
    if "p3" in P.phases:
        emit_ln_t(P, "x1t", "s_xn2", 3)
        emit_conv(P)
    if "p4" in P.phases:
        emit_ln_t(P, "x2t", "s_xn3", 4)
    if "p5" in P.phases:
        emit_ffn_gateup(P, 2)
    elif "p5x" in P.phases:
        emit_ffn_gateup(P, 2, upto=1)
    if "p6" in P.phases:
        emit_ffn_down(P, 2)
    if "p7" in P.phases:
        emit_output(P)
    emit_debug(P)
    for nm, vv in P.cnt.items():
        assert vv < 30000, f"semaphore {nm} total {vv} risks counter wrap"
    return P


def emit_p0_consts(P):
    nc, D, S = P.nc, P.D, P.S
    with nc.Block() as blk:
        @blk.gpsimd
        def _(g):
            for nm in ["ident", "slide", "shift", "ctabr", "ctabi",
                       "e1r", "e1i", "e1in", "twr", "twi", "e2rs", "e2ins"]:
                P.dinc("s_const", g.dma_start(S[nm].ap(), D["c_" + nm].ap()))
            P.dinc("s_const", g.dma_start(S["dcol"].ap(), bass.AP(D["dd"], 0, [[0, 128], [1, 1]])))
            P.dinc("s_const", g.dma_start(S["lamT"].ap(), D["lam"].ap()))
            P.dinc("s_const", g.dma_start(S["priT"].ap(), D["pri"].ap()))
            P.dinc("s_const", g.dma_start(S["qriT"].ap(), D["qri"].ap()))
            P.dinc("s_const", g.dma_start(S["ctT"].ap(), bass.AP(D["ctri"], 0, [[2, 64], [1, 2]])))
            P.dinc("s_const", g.dma_start(S["bpT"].ap(), D["bp"].ap()))
            P.dinc("s_const", g.dma_start(S["lsr"].ap(), bass.AP(D["ls"], 0, [[1, 1], [1, 1]])))
            P.dinc("s_const", g.dma_start(S["vc"].ap(), D["vcri"].ap()))
            g.wait_ge(P.sem("s_const"), P.val("s_const"))


def emit_scales(P):
    nc, D, S = P.nc, P.D, P.S
    for wname, sname, zname, szname, G, N in [
            ("gs", "sg_bf", "gz", "szg_scr", 32, INTER),
            ("us", "su_bf", "uz", "szu_scr", 32, INTER),
            ("ds", "sd_bf", "dz", "szd_scr", 86, HID)]:
        nch = 8 if N == INTER else 4
        csz = N // nch
        zcsz = csz // 8
        base_v = P.val("s_scv")
        base_w = P.val("s_sc_wr")
        with nc.Block() as blk:
            @blk.sync
            def _(sy, wname=wname, zname=zname, G=G, nch=nch, csz=csz, zcsz=zcsz,
                  base_v=base_v, base_w=base_w):
                for c in range(nch):
                    P.w(sy, P.sem("s_scv"), base_v + c)
                    P.w(sy, P.sem("s_sc_wr"), base_w + 32 * c)
                    P.dinc("s_sc_in", sy.dma_start(S["sch_f"][0:G, 0:csz],
                                                   D[wname][:, c * csz:(c + 1) * csz]))
                    P.dinc("s_sc_in", sy.dma_start(S["zch_i"][0:G, 0:zcsz],
                                                   D[zname][:, c * zcsz:(c + 1) * zcsz]))

            @blk.vector
            def _(v, G=G, nch=nch, csz=csz, zcsz=zcsz):
                for c in range(nch):
                    v.wait_ge(P.sem("s_sc_in"), P.val("s_sc_in") - 32 * (nch - 1 - c))
                    P.cinc("s_scb", v.tensor_copy(S["sch_b"][0:G, 0:csz],
                                                  S["sch_f"][0:G, 0:csz]))
                    for s8 in range(8):
                        v.tensor_scalar(
                            S["zch_u"].fullap()[0:G, s8:s8 + 8 * zcsz:8],
                            S["zch_i"][0:G, 0:zcsz], 4 * s8, 15,
                            ALU.logical_shift_right, ALU.bitwise_and)
                    v.drain()
                    v.tensor_scalar(S["zch_f"][0:G, 0:csz], S["zch_u"][0:G, 0:csz],
                                    1.0, -1.0, ALU.add, ALU.mult)
                    v.drain()
                    P.cinc("s_scv", v.tensor_tensor(S["szch"][0:G, 0:csz],
                                                    S["zch_f"][0:G, 0:csz],
                                                    S["sch_b"][0:G, 0:csz], ALU.mult))

            @blk.scalar
            def _(sc, sname=sname, szname=szname, G=G, nch=nch, csz=csz):
                for c in range(nch):
                    sc.wait_ge(P.sem("s_scb"), P.val("s_scb") - (nch - 1 - c))
                    P.dinc("s_sc_wr", sc.dma_start(D[sname][:, c * csz:(c + 1) * csz],
                                                   S["sch_b"][0:G, 0:csz]))
                    sc.wait_ge(P.sem("s_scv"), P.val("s_scv") - (nch - 1 - c))
                    P.dinc("s_sc_wr", sc.dma_start(D[szname][:, c * csz:(c + 1) * csz],
                                                   S["szch"][0:G, 0:csz]))
                P.fence(sc, "s_sc_wr")


def emit_s4(P, upto=3):
    nc, D, S = P.nc, P.D, P.S

    with (nc.psum_tensor("ps4a", [64, 64], F32) as psA,
          nc.psum_tensor("ps4b", [64, 64], F32) as psB,
          nc.psum_tensor("psb1", [64, 1], F32) as psBr,
          nc.psum_tensor("psb2", [64, 1], F32) as psBi,
          nc.Block() as blk):
        @blk.vector
        def _(v):
            v.wait_ge(P.sem("s_const"), P.val("s_const"))
            v.tensor_copy(S["vcr"].ap(), bass.AP(S["vc"], 0, [[128, 64], [2, 64]]))
            P.cinc("s_s4", v.tensor_copy(S["vci"].ap(),
                                         bass.AP(S["vc"], 1, [[128, 64], [2, 64]])))

        @blk.scalar
        def _(sc):
            sc.wait_ge(P.sem("s_const"), P.val("s_const"))
            sc.activation(S["expt"].ap(), S["lsr"].ap(), AF.Exp, scale=-1.0)
            sc.drain()
            sc.mul(S["c2s"].ap(), S["expt"].ap(), 2.0)
            sc.drain()
            P.dinc("s_s4w", sc.dma_start(bass.AP(D["c2s_scr"], 0, [[1, 1], [1, 1]]),
                                         S["c2s"].ap()))
            P.fence(sc, "s_s4w")

        @blk.tensor
        def _(t):
            t.wait_ge(P.sem("s_s4"), 1)
            t.transpose(psA.ap(), S["vcr"].ap(), S["ident"][0:64, 0:64])
            P.cinc("s_s4", t.transpose(psB.ap(), S["vci"].ap(), S["ident"][0:64, 0:64]))

        @blk.vector
        def _(v):
            v.wait_ge(P.sem("s_s4"), 2)
            v.tensor_copy(S["vcrT"].ap(), psA.ap())
            P.cinc("s_s4", v.tensor_copy(S["vciT"].ap(), psB.ap()))

        @blk.tensor
        def _(t):
            t.wait_ge(P.sem("s_s4"), 3)
            t.matmul(psBr.ap(), S["vcrT"].ap(), S["bpT"].ap(), start=True, stop=True)
            P.cinc("s_s4", t.matmul(psBi.ap(), S["vciT"].ap(), S["bpT"].ap(),
                                    start=True, stop=True))

        @blk.vector
        def _(v):
            v.wait_ge(P.sem("s_s4"), 4)
            v.tensor_copy(S["bcol"][:, 0:1], psBr.ap())
            v.tensor_copy(S["bcol"][:, 1:2], psBi.ap())
            br, bi = S["bcol"][:, 0:1], S["bcol"][:, 1:2]
            ctr, cti = S["ctT"][:, 0:1], S["ctT"][:, 1:2]
            pr, pi = S["priT"][:, 0:1], S["priT"][:, 1:2]
            qr, qi = S["qriT"][:, 0:1], S["qriT"][:, 1:2]
            t0, t1, vr, vi = S["vt"]
            for m, (ar, ai, xr, xi) in enumerate([
                    (ctr, cti, br, bi),
                    (ctr, cti, pr, pi),
                    (qr, qi, br, bi),
                    (qr, qi, pr, pi)]):
                v.tensor_tensor(t0.ap(), ar, xr, ALU.mult)
                v.tensor_tensor(t1.ap(), ai, xi, ALU.mult)
                v.drain()
                v.tensor_tensor(vr.ap(), t0.ap(), t1.ap(), ALU.add)
                v.tensor_tensor(t0.ap(), ar, xi, ALU.mult)
                v.tensor_tensor(t1.ap(), ai, xr, ALU.mult)
                v.drain()
                v.tensor_tensor(vi.ap(), t0.ap(), t1.ap(), ALU.subtract)
                v.drain()
                v.tensor_copy(S["vst"][0:64, 2 * m:2 * m + 1], vr.ap())
                v.tensor_copy(S["vst"][64:128, 2 * m:2 * m + 1], vi.ap())
                v.tensor_copy(S["vst"][0:64, 2 * m + 1:2 * m + 2], vi.ap())
                ins = v.tensor_scalar(S["vst"][64:128, 2 * m + 1:2 * m + 2], vr.ap(),
                                      -1.0, None, ALU.mult)
            v.drain()
            P.cinc("s_s4", ins)

    if upto < 2:
        return
    with (nc.psum_tensor("psk8", [8, 512], F32) as pk8,
          nc.Block() as blk):
        marks = []

        @blk.sync
        def _(sy):
            sy.wait_ge(P.sem("s_s4w"), 32)
            P.dinc("s_s4w", sy.dma_start(S["c2scol"].ap(),
                                         bass.AP(D["c2s_scr"], 0, [[0, 64], [1, 1]])))
            for lc in range(8):
                P.w(sy, P.sem("s_s4v"), lc)
                P.dinc("s_s4w", sy.dma_start(
                    S["wbc"][0].ap(), bass.AP(D["c_wtab"], 512 * lc, [[0, 64], [1, 512]])))
                P.dinc("s_s4w", sy.dma_start(
                    S["wbc"][1].ap(), bass.AP(D["c_wtab"], L + 512 * lc, [[0, 64], [1, 512]])))
                P.fence(sy, "s_s4w")
                marks.append(P.val("s_s4w"))

        @blk.vector
        def _(v):
            v.wait_ge(P.sem("s_s4"), 5)
            for lc in range(8):
                v.wait_ge(P.sem("s_s4w"), marks[lc])
                P.w(v, P.sem("s_s4m"), lc)
                v.tensor_scalar(S["dnr"].ap(), S["wbc"][0].ap(),
                                S["c2scol"].ap(), S["lamT"][:, 0:1],
                                ALU.mult, ALU.subtract)
                v.tensor_scalar(S["dni"].ap(), S["wbc"][1].ap(),
                                S["c2scol"].ap(), S["lamT"][:, 1:2],
                                ALU.mult, ALU.subtract)
                v.drain()
                v.tensor_tensor(S["dmag"].ap(), S["dnr"].ap(), S["dnr"].ap(), ALU.mult)
                v.tensor_tensor(S["drec"].ap(), S["dni"].ap(), S["dni"].ap(), ALU.mult)
                v.drain()
                v.tensor_tensor(S["dmag"].ap(), S["dmag"].ap(), S["drec"].ap(), ALU.add)
                v.drain()
                v.reciprocal(S["drec"].ap(), S["dmag"].ap())
                v.drain()
                v.tensor_tensor(S["wbc"][0].ap(), S["dmag"].ap(), S["drec"].ap(), ALU.mult)
                v.drain()
                v.tensor_scalar(S["wbc"][0].ap(), S["wbc"][0].ap(), -1.0, 2.0,
                                ALU.mult, ALU.add)
                v.drain()
                v.tensor_tensor(S["drec"].ap(), S["drec"].ap(), S["wbc"][0].ap(), ALU.mult)
                v.drain()
                v.tensor_tensor(S["rri"][0:64, :], S["dnr"].ap(), S["drec"].ap(), ALU.mult)
                P.cinc("s_s4v", v.tensor_tensor(S["rri"][64:128, :], S["dni"].ap(),
                                                S["drec"].ap(), ALU.mult))

        @blk.tensor
        def _(t):
            for lc in range(8):
                t.wait_ge(P.sem("s_s4v"), lc + 1)
                P.w(t, P.sem("s_s4e"), lc)
                P.cinc("s_s4m", t.matmul(pk8.ap(), S["vst"].ap(), S["rri"].ap(),
                                         start=True, stop=True))

        @blk.scalar
        def _(sc):
            for lc in range(8):
                sc.wait_ge(P.sem("s_s4m"), lc + 1)
                P.w(sc, P.sem("s_s4kc"), 16 * lc)
                sc.copy(S["k8"].ap(), pk8.ap())
                P.cinc("s_s4e", sc.drain())
                P.dinc("s_s4kc", sc.dma_start(
                    bass.AP(D["k8_scr"], 512 * lc, [[L, 8], [1, 512]]), S["k8"].ap()))
            P.fence(sc, "s_s4kc")

    if upto < 3:
        return
    with nc.Block() as blk:
        @blk.sync
        def _(sy):
            sy.wait_ge(P.sem("s_s4kc"), 16 * 9)
            for i in range(8):
                P.dinc("s_s4k", sy.dma_start(
                    S["kt"][i].ap(), bass.AP(D["k8_scr"], i * L, [[32, 128], [1, 32]])))
            P.fence(sy, "s_s4k")

        @blk.vector
        def _(v):
            v.wait_ge(P.sem("s_s4k"), 16 * 9)
            at = S["at"]
            kt = S["kt"]
            v.drain()
            v.tensor_scalar(at[0].ap(), kt[6].ap(), 1.0, None, ALU.add)
            v.drain()
            v.tensor_tensor(at[1].ap(), at[0].ap(), at[0].ap(), ALU.mult)
            v.drain()
            v.tensor_tensor(at[2].ap(), kt[7].ap(), kt[7].ap(), ALU.mult)
            v.drain()
            v.tensor_tensor(at[1].ap(), at[1].ap(), at[2].ap(), ALU.add)
            v.drain()
            v.reciprocal(at[4].ap(), at[1].ap())
            v.drain()
            v.tensor_tensor(at[5].ap(), at[1].ap(), at[4].ap(), ALU.mult)
            v.drain()
            v.tensor_scalar(at[5].ap(), at[5].ap(), -1.0, 2.0, ALU.mult, ALU.add)
            v.drain()
            v.tensor_tensor(at[1].ap(), at[4].ap(), at[5].ap(), ALU.mult)
            v.drain()
            v.drain()
            v.tensor_tensor(at[2].ap(), at[0].ap(), at[1].ap(), ALU.mult)
            v.drain()
            v.tensor_tensor(at[3].ap(), kt[7].ap(), at[1].ap(), ALU.mult)
            v.drain()
            v.tensor_tensor(at[0].ap(), kt[2].ap(), kt[4].ap(), ALU.mult)
            v.drain()
            v.tensor_tensor(at[1].ap(), kt[3].ap(), kt[5].ap(), ALU.mult)
            v.drain()
            v.tensor_tensor(at[0].ap(), at[0].ap(), at[1].ap(), ALU.subtract)
            v.drain()
            v.tensor_tensor(at[1].ap(), kt[2].ap(), kt[5].ap(), ALU.mult)
            v.drain()
            v.tensor_tensor(at[4].ap(), kt[3].ap(), kt[4].ap(), ALU.mult)
            v.drain()
            v.tensor_tensor(at[1].ap(), at[1].ap(), at[4].ap(), ALU.add)
            v.drain()
            v.tensor_tensor(at[4].ap(), at[0].ap(), at[2].ap(), ALU.mult)
            v.drain()
            v.tensor_tensor(at[5].ap(), at[1].ap(), at[3].ap(), ALU.mult)
            v.drain()
            v.tensor_tensor(at[4].ap(), at[4].ap(), at[5].ap(), ALU.add)
            v.drain()
            v.tensor_tensor(at[5].ap(), at[1].ap(), at[2].ap(), ALU.mult)
            v.drain()
            v.tensor_tensor(at[2].ap(), at[0].ap(), at[3].ap(), ALU.mult)
            v.drain()
            v.tensor_tensor(at[5].ap(), at[5].ap(), at[2].ap(), ALU.subtract)
            v.drain()
            v.tensor_tensor(at[0].ap(), kt[0].ap(), at[4].ap(), ALU.subtract)
            v.drain()
            v.tensor_tensor(at[1].ap(), kt[1].ap(), at[5].ap(), ALU.subtract)
            v.drain()
            v.tensor_tensor(at[2].ap(), S["ctabr"].ap(), at[0].ap(), ALU.mult)
            v.drain()
            v.tensor_tensor(at[3].ap(), S["ctabi"].ap(), at[1].ap(), ALU.mult)
            v.drain()
            v.tensor_tensor(at[2].ap(), at[2].ap(), at[3].ap(), ALU.subtract)
            v.drain()
            v.tensor_tensor(at[3].ap(), S["ctabr"].ap(), at[1].ap(), ALU.mult)
            v.drain()
            v.tensor_tensor(at[4].ap(), S["ctabi"].ap(), at[0].ap(), ALU.mult)
            v.drain()
            v.tensor_tensor(at[3].ap(), at[3].ap(), at[4].ap(), ALU.add)
            P.cinc("s_s4a", v.memset(S["zrow"].ap(), 0))

        @blk.sync
        def _(sy):
            sy.wait_ge(P.sem("s_s4a"), 1)
            P.dinc("s_s4k", sy.dma_start(
                bass.AP(D["a_scr"], 0, [[32, 128], [1, 32]]), S["at"][2].ap()))
            P.dinc("s_s4k", sy.dma_start(
                bass.AP(D["a_scr"], L, [[32, 128], [1, 32]]), S["at"][3].ap()))
            sy.wait_ge(P.sem("s_s4k"), P.val("s_s4k"))
            P.dinc("s_s4k", sy.dma_start(
                S["a1r"].ap(), bass.AP(D["a_scr"], 0, [[64, 64], [1, 64]])))
            P.dinc("s_s4k", sy.dma_start(
                S["a1i"].ap(), bass.AP(D["a_scr"], L, [[64, 64], [1, 64]])))
            P.fence(sy, "s_s4k")

    if upto < 4:
        return
    with (nc.psum_tensor("psSr", [64, 64], F32) as pSr,
          nc.psum_tensor("psSi", [64, 64], F32) as pSi,
          nc.psum_tensor("psK2", [64, 64], F32) as pK2,
          nc.Block() as blk):
        @blk.tensor
        def _(t):
            t.wait_ge(P.sem("s_s4k"), P.val("s_s4k"))
            t.matmul(pSr.ap(), S["a1r"].ap(), S["e1r"].ap(), start=True, stop=False)
            t.matmul(pSr.ap(), S["a1i"].ap(), S["e1in"].ap(), start=False, stop=True)
            t.matmul(pSi.ap(), S["a1r"].ap(), S["e1i"].ap(), start=True, stop=False)
            P.cinc("s_s4f", t.matmul(pSi.ap(), S["a1i"].ap(), S["e1r"].ap(),
                                     start=False, stop=True))

        @blk.vector
        def _(v):
            v.wait_ge(P.sem("s_s4f"), 1)
            v.tensor_copy(S["s1r"].ap(), pSr.ap())
            v.tensor_copy(S["s1i"].ap(), pSi.ap())
            v.drain()
            v.tensor_tensor(S["t1r"].ap(), S["s1r"].ap(), S["twr"].ap(), ALU.mult)
            v.tensor_tensor(S["a1r"].ap(), S["s1i"].ap(), S["twi"].ap(), ALU.mult)
            v.drain()
            v.tensor_tensor(S["t1r"].ap(), S["t1r"].ap(), S["a1r"].ap(), ALU.subtract)
            v.tensor_tensor(S["t1i"].ap(), S["s1r"].ap(), S["twi"].ap(), ALU.mult)
            v.tensor_tensor(S["a1i"].ap(), S["s1i"].ap(), S["twr"].ap(), ALU.mult)
            v.drain()
            P.cinc("s_s4f", v.tensor_tensor(S["t1i"].ap(), S["t1i"].ap(), S["a1i"].ap(), ALU.add))

        @blk.tensor
        def _(t):
            t.wait_ge(P.sem("s_s4f"), 2)
            t.matmul(pK2.ap(), S["e2rs"].ap(), S["t1r"].ap(), start=True, stop=False)
            P.cinc("s_s4f", t.matmul(pK2.ap(), S["e2ins"].ap(), S["t1i"].ap(),
                                     start=False, stop=True))

        @blk.scalar
        def _(sc):
            sc.wait_ge(P.sem("s_s4f"), 3)
            sc.copy(S["k2bf"].ap(), pK2.ap())
            P.cinc("s_s4f", sc.drain())

        @blk.sync
        def _(sy):
            sy.wait_ge(P.sem("s_s4f"), 4)
            P.dinc("s_s4k", sy.dma_start(
                bass.AP(D["kpad_scr"], 0, [[128, 1], [1, 128]]), S["zrow"][0:1, 0:128]))
            P.dinc("s_s4k", sy.dma_start(
                bass.AP(D["kpad_scr"], 4224, [[128, 1], [1, 128]]), S["zrow"][0:1, 0:128]))
            P.dinc("s_s4k", sy.dma_start(
                bass.AP(D["kpad_scr"], 128, [[64, 64], [1, 64]]), S["k2bf"].ap()))
            sy.wait_ge(P.sem("s_s4k"), P.val("s_s4k"))
            for p in range(128):
                P.dinc("s_kslide", sy.dma_start(
                    S["kslide"][p:p + 1, :],
                    bass.AP(D["kpad_scr"], 128 - p, [[4224, 1], [1, 4224]])))


def emit_ln1(P):
    nc, D, S = P.nc, P.D, P.S
    with (nc.psum_tensor("ptr0", [128, 128], F32) as pt0,
          nc.psum_tensor("ptr1", [128, 128], F32) as pt1,
          nc.Block() as blk):
        ptr = [pt0, pt1]

        @blk.sync
        def _(sy):
            for tb in range(8):
                P.w(sy, P.sem("s_xn"), tb - 1)
                P.w(sy, P.sem("s_tp"), 32 * (tb - 1))
                P.dinc("s_x_in", sy.dma_start(S["xtile"][tb % 2].ap(),
                                              D["u"][128 * tb:128 * (tb + 1), :]))
                P.fence(sy, "s_x_in")

        @blk.vector
        def _(v):
            for tb in range(8):
                xt = S["xtile"][tb % 2]
                v.wait_ge(P.sem("s_x_in"), 32 * (tb + 1))
                P.w(v, P.sem("s_xn_wr"), 32 * (tb - 1) + 32)
                v.reduce_sum(S["cA"].ap(), xt.ap(), axis=mybir.AxisListType.X)
                v.tensor_tensor(S["xnst_lo"].ap(), xt[:, 0:2048], xt[:, 0:2048], ALU.mult)
                v.tensor_tensor(S["xnst_hi"].ap(), xt[:, 2048:4096], xt[:, 2048:4096],
                                ALU.mult)
                v.drain()
                v.reduce_sum(S["cB"].ap(), S["xnst_lo"].ap(), axis=mybir.AxisListType.X)
                v.reduce_sum(S["cC"].ap(), S["xnst_hi"].ap(), axis=mybir.AxisListType.X)
                v.drain()
                v.tensor_tensor(S["cB"].ap(), S["cB"].ap(), S["cC"].ap(), ALU.add)
                v.tensor_scalar(S["cA"].ap(), S["cA"].ap(), RN, None, ALU.mult)
                v.drain()
                v.tensor_scalar(S["cB"].ap(), S["cB"].ap(), RN, None, ALU.mult)
                v.tensor_tensor(S["cC"].ap(), S["cA"].ap(), S["cA"].ap(), ALU.mult)
                v.drain()
                v.tensor_tensor(S["cB"].ap(), S["cB"].ap(), S["cC"].ap(), ALU.subtract)
                v.drain()
                P.cinc("s_lnv", v.tensor_scalar(S["cB"].ap(), S["cB"].ap(), EPS, None,
                                                ALU.add))
                v.wait_ge(P.sem("s_lns"), tb + 1)
                v.reciprocal(S["cE"].ap(), S["cC"].ap())
                v.drain()
                v.tensor_tensor(S["cC"].ap(), S["cE"].ap(), S["cE"].ap(), ALU.mult)
                v.drain()
                v.tensor_tensor(S["cC"].ap(), S["cB"].ap(), S["cC"].ap(), ALU.mult)
                v.drain()
                v.tensor_scalar(S["cC"].ap(), S["cC"].ap(), -1.0, 3.0, ALU.mult, ALU.add)
                v.drain()
                v.scalar_tensor_tensor(S["cE"].ap(), S["cE"].ap(), 0.5, S["cC"].ap(),
                                       ALU.mult, ALU.mult)
                v.drain()
                v.tensor_scalar(S["xnst_lo"].ap(), xt[:, 0:2048],
                                S["cA"].ap(), S["cE"].ap(), ALU.subtract, ALU.mult)
                P.cinc("s_xn", v.tensor_scalar(S["xnst_hi"].ap(), xt[:, 2048:4096],
                                               S["cA"].ap(), S["cE"].ap(),
                                               ALU.subtract, ALU.mult))

        @blk.scalar
        def _(sc):
            for tb in range(8):
                sc.wait_ge(P.sem("s_lnv"), tb + 1)
                sc.activation(S["cC"].ap(), S["cB"].ap(), AF.Sqrt)
                P.cinc("s_lns", sc.drain())
                sc.wait_ge(P.sem("s_xn"), tb + 1)
                P.dinc("s_xn_wr", sc.dma_start(
                    D["xn1_scr"][128 * tb:128 * (tb + 1), 0:2048], S["xnst_lo"].ap()))
                P.dinc("s_xn_wr", sc.dma_start(
                    D["xn1_scr"][128 * tb:128 * (tb + 1), 2048:4096], S["xnst_hi"].ap()))
                for j in range(KJ):
                    idx = 32 * tb + j
                    sc.wait_ge(P.sem("s_tp"), idx + 1)
                    sc.copy(S["x0st"][idx % 2].ap(), ptr[idx % 2].ap())
                    P.cinc("s_tpe", sc.drain())
                    P.dinc("s_x0t_wr", sc.dma_start(
                        D["x0t"][j, :, 128 * tb:128 * (tb + 1)], S["x0st"][idx % 2].ap()))
                    if idx >= 1:
                        sc.wait_ge(P.sem("s_x0t_wr"), P.val("s_x0t_wr") - 16)
            P.fence(sc, "s_xn_wr")
            P.fence(sc, "s_x0t_wr")

        @blk.tensor
        def _(t):
            for tb in range(8):
                t.wait_ge(P.sem("s_x_in"), 32 * (tb + 1))
                for j in range(KJ):
                    idx = 32 * tb + j
                    P.w(t, P.sem("s_tpe"), idx - 1)
                    P.cinc("s_tp", t.transpose(ptr[idx % 2].ap(),
                                               S["xtile"][tb % 2][:, 128 * j:128 * (j + 1)],
                                               S["ident"].ap()))

    with nc.Block() as blk:
        @blk.sync
        def _(sy):
            sy.wait_ge(P.sem("s_xn_wr"), P.val("s_xn_wr"))
            for j in range(KJ):
                P.dinc("s_xnt", sy.dma_start_transpose(
                    S["xnT"][:, T * j:T * (j + 1)],
                    D["xn1_scr"][:, 128 * j:128 * (j + 1)]))
            P.dinc("s_xnt", sy.dma_start_transpose(
                S["hbf"][0].ap(), D["xn1_scr"][:, 0:128]))


def emit_ffn_gateup(P, pas, upto=2):
    nc, D, S = P.nc, P.D, P.S
    sfx = "" if pas == 1 else "5"
    xnt_sem, xnt_val = (("s_xnt", 16 * (KJ + 1)) if pas == 1 else ("s_xn3", KJ))
    s_qin, s_mm = "s_qin" + sfx, "s_mm" + sfx
    s_hrdy, s_hwr, s_h1 = "s_hrdy" + sfx, "s_hwr" + sfx, "s_h1" + sfx
    s_xgmm, s_dqv = "s_xgmm" + sfx, "s_dqv" + sfx

    with (nc.psum_tensor("pxg" + sfx, [32, T], F32) as pxg,
          nc.Block() as blk):
        @blk.tensor
        def _(t):
            t.wait_ge(P.sem(xnt_sem), xnt_val)
            for j in range(KJ):
                for th in range(2):
                    ins = t.matmul(pxg[:, HT * th:HT * (th + 1)],
                                   S["slide"][:, 85 - j:117 - j],
                                   S["xnT"][:, T * j + HT * th:T * j + HT * (th + 1)],
                                   start=(j == 0), stop=(j == KJ - 1))
            P.cinc(s_xgmm, ins)

        @blk.vector
        def _(v):
            v.wait_ge(P.sem(s_xgmm), 1)
            v.tensor_copy(S["xg"].ap(), pxg.ap())
            v.drain()
            P.cinc(s_xgmm, v.tensor_tensor(S["xgl"].ap(), pxg.ap(), S["xg"].ap(),
                                           ALU.subtract))

    if upto < 2:
        return
    with (nc.psum_tensor("pg0" + sfx, [128, T], F32) as pg0,
          nc.psum_tensor("pg1" + sfx, [128, T], F32) as pg1,
          nc.psum_tensor("pu0" + sfx, [128, T], F32) as pu0,
          nc.psum_tensor("pu1" + sfx, [128, T], F32) as pu1,
          nc.Block() as blk):
        pg = [pg0, pg1]
        pu = [pu0, pu1]

        @blk.sync
        def _(sy):
            if pas == 1:
                sy.wait_ge(P.sem("s_sc_wr"), P.val("s_sc_wr"))
                for nt in range(NT):
                    P.w(sy, P.sem(s_mm), nt - 1)
                    P.dinc("s_szin" + sfx, sy.dma_start(
                        S["szt_g"][nt % 2].ap(), D["szg_scr"][:, 128 * nt:128 * (nt + 1)]))
                    P.dinc("s_szin" + sfx, sy.dma_start(
                        S["szt_u"][nt % 2].ap(), D["szu_scr"][:, 128 * nt:128 * (nt + 1)]))
                    qsem = s_qin + "e%d" % (nt // 16)
                    for gu in range(2):
                        qsrc = D["gq"] if gu == 0 else D["uq"]
                        ssrc = D["sg_bf"] if gu == 0 else D["su_bf"]
                        for jq in range(8):
                            idx = nt * 16 + gu * 8 + jq
                            slot = idx % 6
                            P.w(sy, P.sem(s_dqv), idx - 5)
                            for jj in range(4):
                                j = 4 * jq + jj
                                sy.dma_start(
                                    S["qbuf"][slot][:, 128 * jj:128 * (jj + 1)],
                                    bass.AP(qsrc, (16 * j) * INTER + 128 * nt,
                                            [[INTER, 16], [0, 8], [1, 128]])).then_inc(
                                    P.sem(qsem), 16)
                                P.cnt[qsem] += 16
                            P.dinc(qsem, sy.dma_start(
                                S["sbc"][slot].ap(),
                                bass.AP(ssrc, (4 * jq) * INTER + 128 * nt,
                                        [[0, 128], [INTER, 4], [1, 128]])))
                            P.marks[("q1", idx)] = (qsem, P.val(qsem))
                lastsem = s_qin + "e%d" % ((NT - 1) // 16)
                for f in range(2):
                    P.fence(sy, lastsem)
                    P.marks[("q1", NT * 16 + f)] = (lastsem, P.val(lastsem))
            else:
                sy.wait_ge(P.sem("s_wwr"), P.val("s_wwr"))
                for nt in range(NT):
                    P.w(sy, P.sem(s_mm), nt - 1)
                    P.dinc("s_szin" + sfx, sy.dma_start(
                        S["szt_g"][nt % 2].ap(), D["szg_scr"][:, 128 * nt:128 * (nt + 1)]))
                    P.dinc("s_szin" + sfx, sy.dma_start(
                        S["szt_u"][nt % 2].ap(), D["szu_scr"][:, 128 * nt:128 * (nt + 1)]))
                    P.dinc(s_qin, sy.dma_start(S["chunk"][nt % 2].ap(),
                                               D["wgu_scr"][nt]))
                P.fence(sy, s_qin)

        @blk.vector
        def _(v):
            if pas == 1:
                pend = []

                def emit_tt(v):
                    k, kpar, kgu, kjq = pend.pop(0)
                    col = (kgu * KJ + 4 * kjq) * 128
                    P.cinc(s_dqv, v.tensor_tensor(
                        S["chunk"][kpar][:, col:col + 512],
                        S["w4i"][k % 3].ap(),
                        S["sbc"][k % 6].ap(), ALU.mult))

                IMAX = NT * 16 + 1
                for nt in range(NT):
                    par = nt % 2
                    P.w(v, P.sem(s_mm), nt - 1)
                    P.w(v, P.sem("s_wwr"), 16 * (nt - 1))
                    for gu in range(2):
                        for jq in range(8):
                            idx = nt * 16 + gu * 8 + jq
                            slot = idx % 6
                            if len(pend) >= 2:
                                emit_tt(v)
                            qs, qv = P.marks[("q1", idx)]
                            v.wait_ge(P.sem(qs), qv)
                            qs2, qv2 = P.marks[("q1", min(idx + 2, IMAX))]
                            if qs2 != qs:
                                v.wait_ge(P.sem(qs2), qv2)
                            else:
                                v.wait_ge(P.sem(qs), qv2)
                            v.tensor_scalar(S["w4i"][idx % 3].ap(),
                                            S["qbuf"][slot].ap(),
                                            S["shift"].ap(), 15,
                                            ALU.logical_shift_right, ALU.bitwise_and)
                            pend.append((idx, par, gu, jq))
                    if nt >= 1:
                        emit_h_mult(P, v, nt - 1, sfx, pu)
                while pend:
                    emit_tt(v)
                emit_h_mult(P, v, NT - 1, sfx, pu)
            else:
                for nt in range(NT):
                    emit_h_mult(P, v, nt, sfx, pu)

        @blk.tensor
        def _(t):
            t.wait_ge(P.sem(s_xgmm), 2)
            for nt in range(NT):
                par = nt % 2
                if pas == 1:
                    t.wait_ge(P.sem(s_dqv), 16 * (nt + 1))
                else:
                    t.wait_ge(P.sem(s_qin), 16 * min(nt + 2, NT + 1))
                t.wait_ge(P.sem("s_szin" + sfx), 32 * (nt + 1))
                P.w(t, P.sem(s_hrdy), nt - 1)
                for gu in range(2):
                    pp = pg[par] if gu == 0 else pu[par]
                    sz = S["szt_g"][par] if gu == 0 else S["szt_u"][par]
                    co = 0
                    for th in range(2):
                        for j in range(KJ):
                            t.matmul(pp[:, HT * th:HT * (th + 1)],
                                     S["chunk"][par][:, (gu * KJ + j) * 128:(gu * KJ + j + 1) * 128],
                                     S["xnT"][:, T * j + HT * th:T * j + HT * th + HT],
                                     start=(j == 0), stop=False)
                        t.matmul(pp[:, HT * th:HT * (th + 1)],
                                 sz.ap(),
                                 S["xg"][:, HT * th:HT * th + HT],
                                 start=False, stop=False)
                        ins = t.matmul(pp[:, HT * th:HT * (th + 1)],
                                       sz.ap(),
                                       S["xgl"][:, HT * th:HT * th + HT],
                                       start=False, stop=True)
                P.cinc(s_mm, ins)

        @blk.scalar
        def _(sc):
            for nt in range(NT):
                par = nt % 2
                sc.wait_ge(P.sem(s_mm), nt + 1)
                sc.activation(S["hg"][par].ap(), pg[par].ap(), AF.Silu)
                P.cinc(s_h1, sc.drain())
                if pas == 1:
                    P.dinc("s_wwr", sc.dma_start(D["wgu_scr"][nt],
                                                 S["chunk"][par].ap()))
                sc.wait_ge(P.sem(s_hrdy), nt + 1)
                P.dinc(s_hwr, sc.dma_start(D["h_scr"][nt], S["hbf"][par].ap()))

    if pas == 1:
        emit_down_dequant(P)


def emit_h_mult(P, v, nt, sfx, pu):
    S = P.S
    par = nt % 2
    v.wait_ge(P.sem("s_h1" + sfx), nt + 1)
    P.w(v, P.sem("s_hwr" + sfx), 16 * (nt - 1))
    P.cinc("s_hrdy" + sfx,
           v.tensor_tensor(S["hbf"][par].ap(), S["hg"][par].ap(), pu[par].ap(), ALU.mult))


def emit_down_dequant(P):
    nc, D, S = P.nc, P.D, P.S
    with nc.Block() as blk:
        @blk.sync
        def _(sy):
            for k in range(KD):
                qsem = "s_dqin_e%d" % (k // 22)
                for c4 in range(8):
                    idx = k * 8 + c4
                    slot = idx % 6
                    P.w(sy, P.sem("s_ddqv"), idx - 5)
                    P.dinc(qsem, sy.dma_start(
                        S["qbuf"][slot][:, 0:512],
                        bass.AP(D["dq"], (16 * k) * HID + 512 * c4,
                                [[HID, 16], [0, 8], [1, 512]])))
                    P.dinc(qsem, sy.dma_start(
                        S["sbc"][slot][:, 0:512],
                        bass.AP(D["sd_bf"], k * HID + 512 * c4, [[0, 128], [1, 512]])))
                    P.marks[("dq", idx)] = (qsem, P.val(qsem))
            qsem = "s_dqin_e%d" % ((KD - 1) // 22)
            for f in range(2):
                P.fence(sy, qsem)
                P.marks[("dq", KD * 8 + f)] = (qsem, P.val(qsem))

        @blk.vector
        def _(v):
            pend = []
            DMAX = KD * 8 + 1

            def emit_tt(v):
                kk = pend.pop(0)
                P.w(v, P.sem("s_wdwr"), 16 * (kk - 1))
                P.cinc("s_ddqv", v.tensor_tensor(S["wds"][kk % 2].ap(),
                                                 S["w4d"][kk % 3].ap(),
                                                 S["sbc"][kk % 6][:, 0:512], ALU.mult))

            for k in range(KD):
                for c4 in range(8):
                    idx = k * 8 + c4
                    slot = idx % 6
                    if len(pend) >= 2:
                        emit_tt(v)
                    qs, qv = P.marks[("dq", idx)]
                    v.wait_ge(P.sem(qs), qv)
                    qs2, qv2 = P.marks[("dq", min(idx + 2, DMAX))]
                    if qs2 != qs:
                        v.wait_ge(P.sem(qs2), qv2)
                    else:
                        v.wait_ge(P.sem(qs), qv2)
                    v.tensor_scalar(S["w4d"][idx % 3].ap(), S["qbuf"][slot][:, 0:512],
                                    S["shift"].ap(), 15,
                                    ALU.logical_shift_right, ALU.bitwise_and)
                    pend.append(idx)
            while pend:
                emit_tt(v)

        @blk.scalar
        def _(sc):
            for k in range(KD):
                for c4 in range(8):
                    idx = k * 8 + c4
                    sc.wait_ge(P.sem("s_ddqv"), idx + 1)
                    P.dinc("s_wdwr", sc.dma_start(
                        D["wd_scr"][k][:, 512 * c4:512 * (c4 + 1)],
                        S["wds"][idx % 2].ap()))
            P.fence(sc, "s_wdwr")


def emit_ffn_down(P, pas, upto=2):
    nc, D, S = P.nc, P.D, P.S
    sfx = "" if pas == 1 else "6"
    src_t = "x0t" if pas == 1 else "x2t"
    dst_t = "x1t" if pas == 1 else "x3t"
    s_hin, s_wdin, s_hgmm = "s_hin" + sfx, "s_wdin" + sfx, "s_hgmm" + sfx
    s_dmm, s_dz, s_dev = "s_dmm" + sfx, "s_dz" + sfx, "s_dev" + sfx
    s_rin, s_xwr = "s_rin" + sfx, "s_xwr" + sfx
    h_sem = "s_hwr" if pas == 1 else "s_hwr5"
    h_w_total = P.val(h_sem)
    wd_total = P.val("s_wdwr")
    src_sem, src_total = (("s_x0t_wr", P.val("s_x0t_wr")) if pas == 1
                          else ("s_x2wr", P.val("s_x2wr")))

    with (nc.psum_tensor("phg" + sfx, [86, T], F32) as phg,
          nc.Block() as blk):
        @blk.sync
        def _(sy):
            sy.wait_ge(P.sem(h_sem), h_w_total)
            for k in range(KD):
                P.w(sy, P.sem(s_hgmm), k - 3)
                P.dinc(s_hin, sy.dma_start(S["hstr"][k % 4].ap(), D["h_scr"][k]))
            P.fence(sy, s_hin)

        @blk.tensor
        def _(t):
            for k in range(KD):
                t.wait_ge(P.sem(s_hin), 16 * min(k + 3, KD + 1))
                for th in range(2):
                    ins = t.matmul(phg[:, HT * th:HT * (th + 1)],
                                   S["slide"][:, 85 - k:171 - k],
                                   S["hstr"][k % 4][:, HT * th:HT * th + HT],
                                   start=(k == 0), stop=(k == KD - 1))
                P.cinc(s_hgmm, ins)

        @blk.vector
        def _(v):
            v.wait_ge(P.sem(s_hgmm), KD)
            v.tensor_copy(S["hgr"].ap(), phg.ap())
            v.drain()
            P.cinc(s_hgmm, v.tensor_tensor(S["hgl"].ap(), phg.ap(), S["hgr"].ap(),
                                           ALU.subtract))

    if upto < 2:
        return
    with (nc.psum_tensor("pd0" + sfx, [128, T], F32) as pd0,
          nc.psum_tensor("pd1" + sfx, [128, T], F32) as pd1,
          nc.psum_tensor("pd2" + sfx, [128, T], F32) as pd2,
          nc.psum_tensor("pd3" + sfx, [128, T], F32) as pd3,
          nc.Block() as blk):
        pdn = [pd0, pd1, pd2, pd3]

        @blk.sync
        def _(sy):
            if upto == 5:
                for grp in range(8):
                    for i in range(4):
                        nt = 4 * grp + i
                        P.dinc(s_rin, sy.dma_start(S["xres4"][nt % 4].ap(), D[src_t][nt]))
                        P.dinc("s_szdin" + sfx, sy.dma_start(
                            S["szdt"][nt % 4].ap(), D["szd_scr"][:, 128 * nt:128 * (nt + 1)]))
                    for k in range(KD):
                        idx = grp * KD + k
                        P.dinc(s_wdin, sy.dma_start(
                            S["wd4"][idx % 4].ap(),
                            D["wd_scr"][k][:, 512 * grp:512 * (grp + 1)]))
                        P.dinc(s_hin, sy.dma_start(S["hstr"][idx % 4].ap(),
                                                   D["h_scr"][k]))
                sy.wait_ge(P.sem(s_wdin), P.val(s_wdin))
                sy.wait_ge(P.sem(s_hin), P.val(s_hin))
                sy.wait_ge(P.sem(s_rin), P.val(s_rin))
                sy.wait_ge(P.sem("s_szdin" + sfx), P.val("s_szdin" + sfx))
                return
            if pas == 1:
                sy.wait_ge(P.sem("s_wdwr"), wd_total)
            sy.wait_ge(P.sem(src_sem), src_total)
            sy.wait_ge(P.sem(s_hgmm), KD + 1)
            for grp in range(8):
                for k in range(KD):
                    idx = grp * KD + k
                    P.w(sy, P.sem(s_dmm), idx - 3)
                    P.dinc(s_wdin, sy.dma_start(
                        S["wd4"][idx % 4].ap(),
                        D["wd_scr"][k][:, 512 * grp:512 * (grp + 1)]))
                    P.dinc(s_hin, sy.dma_start(S["hstr"][idx % 4].ap(),
                                               D["h_scr"][k]))
                for i in range(4):
                    nt = 4 * grp + i
                    P.w(sy, P.sem(s_dev), nt - 3)
                    P.dinc(s_rin, sy.dma_start(S["xres4"][nt % 4].ap(), D[src_t][nt]))
                    P.w(sy, P.sem(s_dz), grp)
                    P.dinc("s_szdin" + sfx, sy.dma_start(
                        S["szdt"][nt % 4].ap(), D["szd_scr"][:, 128 * nt:128 * (nt + 1)]))
            P.fence(sy, s_wdin)
            P.fence(sy, s_hin)

        @blk.tensor
        def _(t):
            if upto == 5:
                return
            if upto == 4:
                for grp in range(8):
                    for k in range(KD):
                        idx = grp * KD + k
                        t.wait_ge(P.sem(s_wdin), 16 * min(idx + 3, 8 * KD + 1))
                        t.wait_ge(P.sem(s_hin), 16 * (KD + 1) + 16 * min(idx + 3, 8 * KD + 1))
                        P.cinc(s_dmm, t.nop())
                    P.cinc(s_dz, t.nop())
                return
            NOZ = (upto == 3)
            base_h = 16 * (KD + 1)
            for grp in range(8):
                P.w(t, P.sem(s_dev), 4 * grp)
                for k in range(KD):
                    idx = grp * KD + k
                    t.wait_ge(P.sem(s_wdin), 16 * min(idx + 3, 8 * KD + 1))
                    t.wait_ge(P.sem(s_hin), base_h + 16 * min(idx + 3, 8 * KD + 1))
                    for i in range(4):
                        for th in range(2):
                            ins = t.matmul(pdn[i][:, HT * th:HT * (th + 1)],
                                           S["wd4"][idx % 4][:, 128 * i:128 * (i + 1)],
                                           S["hstr"][idx % 4][:, HT * th:HT * th + HT],
                                           start=(k == 0),
                                           stop=(NOZ and k == KD - 1))
                    P.cinc(s_dmm, ins)
                if NOZ:
                    P.cinc(s_dz, t.nop())
                    continue
                if True:
                    for i in range(4):
                        nt = 4 * grp + i
                        t.wait_ge(P.sem("s_szdin" + sfx), 16 * (nt + 1))
                        for th in range(2):
                            t.matmul(pdn[i][:, HT * th:HT * (th + 1)],
                                     S["szdt"][nt % 4].ap(),
                                     S["hgr"][:, HT * th:HT * th + HT],
                                     start=False, stop=False)
                            ins = t.matmul(pdn[i][:, HT * th:HT * (th + 1)],
                                           S["szdt"][nt % 4].ap(),
                                           S["hgl"][:, HT * th:HT * th + HT],
                                           start=False, stop=True)
                P.cinc(s_dz, ins)
                del ins

        @blk.vector
        def _(v):
            if upto == 5:
                return
            for grp in range(8):
                v.wait_ge(P.sem(s_dz), grp + 1)
                for i in range(4):
                    nt = 4 * grp + i
                    v.wait_ge(P.sem(s_rin), 16 * (nt + 1))
                    P.w(v, P.sem(s_xwr), 16 * (nt - 1))
                    if upto == 4:
                        P.cinc(s_dev, v.tensor_copy(S["xout"][nt % 2].ap(),
                                                    S["xres4"][nt % 4].ap()))
                    else:
                        P.cinc(s_dev, v.scalar_tensor_tensor(
                            S["xout"][nt % 2].ap(), pdn[i].ap(), 0.5,
                            S["xres4"][nt % 4].ap(), ALU.mult, ALU.add))

        @blk.scalar
        def _(sc):
            if upto == 5:
                return
            for grp in range(8):
                for i in range(4):
                    nt = 4 * grp + i
                    sc.wait_ge(P.sem(s_dev), nt + 1)
                    P.dinc(s_xwr, sc.dma_start(D[dst_t][nt], S["xout"][nt % 2].ap()))
            P.fence(sc, s_xwr)


def emit_ln_t(P, src_t, dst_sem, pas):
    nc, D, S = P.nc, P.D, P.S
    sfx = str(pas)
    s_in, s_bv, s_smm, s_row = "s_li" + sfx, "s_lb" + sfx, "s_lm" + sfx, "s_lr" + sfx
    xwr_sem = "s_xwr" if pas == 3 else "s_x2wr"
    xwr_total = P.val(xwr_sem)

    with (nc.psum_tensor("psx0" + sfx, [1, HT], F32) as px0,
          nc.psum_tensor("psx1" + sfx, [1, HT], F32) as px1,
          nc.psum_tensor("psq0" + sfx, [1, HT], F32) as pq0,
          nc.psum_tensor("psq1" + sfx, [1, HT], F32) as pq1,
          nc.Block() as blk):
        px = [px0, px1]
        pq = [pq0, pq1]

        @blk.sync
        def _(sy):
            sy.wait_ge(P.sem(xwr_sem), xwr_total)
            for nt in range(ND):
                P.w(sy, P.sem(s_bv), nt - 3)
                P.dinc(s_in, sy.dma_start(S["xres4"][nt % 4].ap(), D[src_t][nt]))
            P.fence(sy, s_in)

        @blk.vector
        def _(v):
            for nt in range(ND):
                v.wait_ge(P.sem(s_in), 16 * min(nt + 3, ND + 1))
                P.w(v, P.sem(s_smm), nt)
                v.tensor_copy(S["xbf"].ap(), S["xres4"][nt % 4].ap())
                v.drain()
                P.cinc(s_bv, v.tensor_tensor(S["xsq"].ap(), S["xbf"].ap(), S["xbf"].ap(),
                                             ALU.mult))

        @blk.tensor
        def _(t):
            for nt in range(ND):
                t.wait_ge(P.sem(s_bv), nt + 1)
                for th in range(2):
                    t.matmul(px[th].ap(), S["slide"][:, 85:86],
                             S["xbf"][:, HT * th:HT * th + HT],
                             start=(nt == 0), stop=(nt == ND - 1))
                    ins = t.matmul(pq[th].ap(), S["slide"][:, 85:86],
                                   S["xsq"][:, HT * th:HT * th + HT],
                                   start=(nt == 0), stop=(nt == ND - 1))
                P.cinc(s_smm, ins)

        @blk.vector
        def _(v):
            v.wait_ge(P.sem(s_smm), ND)
            for th in range(2):
                v.tensor_copy(S["st_m"][:, HT * th:HT * th + HT], px[th].ap())
                v.tensor_copy(S["st_q"][:, HT * th:HT * th + HT], pq[th].ap())
            v.drain()
            v.tensor_scalar(S["st_m"].ap(), S["st_m"].ap(), RN, None, ALU.mult)
            v.tensor_scalar(S["st_q"].ap(), S["st_q"].ap(), RN, None, ALU.mult)
            v.drain()
            v.tensor_tensor(S["st_v"].ap(), S["st_m"].ap(), S["st_m"].ap(), ALU.mult)
            v.drain()
            v.tensor_tensor(S["st_v"].ap(), S["st_q"].ap(), S["st_v"].ap(), ALU.subtract)
            v.drain()
            P.cinc(s_row, v.tensor_scalar(S["st_v"].ap(), S["st_v"].ap(), EPS, None,
                                          ALU.add))
            v.wait_ge(P.sem(s_row), 2)
            v.reciprocal(S["st_r"].ap(), S["st_q"].ap())
            v.drain()
            v.tensor_tensor(S["st_q"].ap(), S["st_r"].ap(), S["st_r"].ap(), ALU.mult)
            v.drain()
            v.tensor_tensor(S["st_q"].ap(), S["st_v"].ap(), S["st_q"].ap(), ALU.mult)
            v.drain()
            v.tensor_scalar(S["st_q"].ap(), S["st_q"].ap(), -1.0, 3.0, ALU.mult, ALU.add)
            v.drain()
            v.scalar_tensor_tensor(S["st_r"].ap(), S["st_r"].ap(), 0.5, S["st_q"].ap(),
                                   ALU.mult, ALU.mult)
            v.drain()
            P.cinc(s_row, v.tensor_tensor(S["st_m"].ap(), S["st_m"].ap(), S["st_r"].ap(),
                                          ALU.mult))

        @blk.scalar
        def _(sc):
            sc.wait_ge(P.sem(s_row), 1)
            sc.activation(S["st_q"].ap(), S["st_v"].ap(), AF.Sqrt)
            P.cinc(s_row, sc.drain())

        @blk.sync
        def _(sy):
            sy.wait_ge(P.sem(s_row), 3)
            P.dinc(s_row + "d", sy.dma_start(bass.AP(D["row_scr"], 0, [[T, 1], [1, T]]),
                                             S["st_r"].ap()))
            P.dinc(s_row + "d", sy.dma_start(bass.AP(D["row_scr"], T, [[T, 1], [1, T]]),
                                             S["st_m"].ap()))
            sy.wait_ge(P.sem(s_row + "d"), P.val(s_row + "d"))
            P.dinc(s_row + "d", sy.dma_start(S["r_bc"].ap(),
                                             bass.AP(D["row_scr"], 0, [[0, 128], [1, T]])))
            P.dinc(s_row + "d", sy.dma_start(S["mr_bc"].ap(),
                                             bass.AP(D["row_scr"], T, [[0, 128], [1, T]])))
            P.fence(sy, s_row + "d")

    rowd_total = P.val(s_row + "d")
    base_in = P.val(s_in)
    with nc.Block() as blk:
        @blk.sync
        def _(sy):
            for nt in range(ND):
                P.w(sy, P.sem(dst_sem), nt - 3)
                P.dinc(s_in, sy.dma_start(S["xres4"][nt % 4].ap(), D[src_t][nt]))
            P.fence(sy, s_in)

        @blk.vector
        def _(v):
            v.wait_ge(P.sem(s_row + "d"), rowd_total)
            for nt in range(ND):
                v.wait_ge(P.sem(s_in), base_in + 16 * min(nt + 3, ND + 1))
                v.tensor_tensor(S["lntmp"].ap(), S["xres4"][nt % 4].ap(), S["r_bc"].ap(),
                                ALU.mult)
                v.drain()
                P.cinc(dst_sem, v.tensor_tensor(S["xnT"][:, T * nt:T * (nt + 1)],
                                                S["lntmp"].ap(), S["mr_bc"].ap(),
                                                ALU.subtract))


def emit_conv(P):
    nc, D, S = P.nc, P.D, P.S
    with (nc.psum_tensor("pcv0", [128, T], F32) as pc0,
          nc.psum_tensor("pcv1", [128, T], F32) as pc1,
          nc.Block() as blk):
        pc = [pc0, pc1]

        @blk.sync
        def _(sy):
            for it in range(ND):
                P.w(sy, P.sem("s_cev"), it - 1)
                P.dinc("s_cin", sy.dma_start(S["xres"][it % 2].ap(), D["x1t"][it]))
            P.fence(sy, "s_cin")

        @blk.tensor
        def _(t):
            t.wait_ge(P.sem("s_kslide"), 16 * 128)
            for it in range(ND):
                t.wait_ge(P.sem("s_xn2"), it + 1)
                P.w(t, P.sem("s_cev"), it - 1)
                for th in range(2):
                    for jt in range(it + 1):
                        dd = it - jt
                        ins = t.matmul(pc[it % 2][:, HT * th:HT * (th + 1)],
                                       S["kslide"][:, 128 * dd:128 * (dd + 1)],
                                       S["xnT"][:, T * jt + HT * th:T * jt + HT * th + HT],
                                       start=(jt == 0), stop=(jt == it))
                P.cinc("s_cmm", ins)

        @blk.vector
        def _(v):
            for it in range(ND):
                v.wait_ge(P.sem("s_cmm"), it + 1)
                v.wait_ge(P.sem("s_cin"), 16 * min(it + 2, ND + 1))
                P.w(v, P.sem("s_x2wr"), 16 * (it - 1))
                v.scalar_tensor_tensor(S["hg"][1].ap(),
                                       S["xnT"][:, T * it:T * (it + 1)],
                                       S["dcol"].ap(), pc[it % 2].ap(),
                                       ALU.mult, ALU.add)
                P.cinc("s_cev", v.tensor_tensor(S["xout"][it % 2].ap(), S["hg"][1].ap(),
                                                S["xres"][it % 2].ap(), ALU.add))

        @blk.scalar
        def _(sc):
            for it in range(ND):
                sc.wait_ge(P.sem("s_cev"), it + 1)
                P.dinc("s_x2wr", sc.dma_start(D["x2t"][it], S["xout"][it % 2].ap()))
            P.fence(sc, "s_x2wr")


def emit_output(P):
    nc, D, S = P.nc, P.D, P.S
    xwr_total = P.val("s_xwr6")
    with (nc.psum_tensor("pot0", [128, 128], F32) as po0,
          nc.psum_tensor("pot1", [128, 128], F32) as po1,
          nc.Block() as blk):
        po = [po0, po1]

        @blk.sync
        def _(sy):
            sy.wait_ge(P.sem("s_xwr6"), xwr_total)
            for nt in range(ND):
                P.w(sy, P.sem("s_otp"), 8 * (nt - 1))
                P.dinc("s_oin", sy.dma_start(S["xres"][nt % 2].ap(), D["x3t"][nt]))
            P.fence(sy, "s_oin")

        @blk.tensor
        def _(t):
            for nt in range(ND):
                t.wait_ge(P.sem("s_oin"), 16 * min(nt + 2, ND + 1))
                for tb in range(8):
                    idx = 8 * nt + tb
                    P.w(t, P.sem("s_ote"), idx - 1)
                    P.cinc("s_otp", t.transpose(
                        po[idx % 2].ap(),
                        S["xres"][nt % 2][:, 128 * tb:128 * (tb + 1)],
                        S["ident"].ap()))

        @blk.scalar
        def _(sc):
            for nt in range(ND):
                for tb in range(8):
                    idx = 8 * nt + tb
                    sc.wait_ge(P.sem("s_otp"), idx + 1)
                    sc.copy(S["ot"][idx % 2].ap(), po[idx % 2].ap())
                    P.cinc("s_ote", sc.drain())
                    P.dinc("s_owr", sc.dma_start(
                        D["out"][128 * tb:128 * (tb + 1), 128 * nt:128 * (nt + 1)],
                        S["ot"][idx % 2].ap()))
                    sc.wait_ge(P.sem("s_owr"), P.val("s_owr") - 16)
            P.fence(sc, "s_owr")
            sc.wait_ge(P.sem("s_owr"), P.val("s_owr"))


def emit_debug(P):
    nc, D, S = P.nc, P.D, P.S
    taps = P.debug_taps
    if not taps:
        return
    waits = [(nm, v) for nm, v in sorted(P.cnt.items()) if v > 0]
    with nc.Block() as blk:
        @blk.gpsimd
        def _(g):
            for nm, v in waits:
                g.wait_ge(P.sems[nm], v)
            for name in sorted(taps):
                if name in D:
                    srct = D[name]
                else:
                    srct = S[name]
                dst = nc.dram_tensor("dbg_" + name, list(srct.shape), srct.dtype,
                                     kind="ExternalOutput")
                P.dinc("s_dbg", g.dma_start(dst.ap(), srct.ap()))
            g.wait_ge(P.sem("s_dbg"), P.val("s_dbg"))


def make_in_maps(inputs):
    consts = host_consts()
    u_full = np.ascontiguousarray(np.asarray(inputs["L0_attn_u"], np.float32).reshape(8192, HID))
    base = {
        "lam": np.ascontiguousarray(inputs["L0_attn_Lambda_ri"], np.float32),
        "pri": np.ascontiguousarray(inputs["L0_attn_p_ri"], np.float32),
        "qri": np.ascontiguousarray(inputs["L0_attn_q_ri"], np.float32),
        "vcri": np.ascontiguousarray(np.asarray(inputs["L0_attn_Vc_ri"], np.float32).reshape(64, 128)),
        "ctri": np.ascontiguousarray(np.asarray(inputs["L0_attn_Ct_ri"], np.float32).reshape(128)),
        "bp": np.ascontiguousarray(inputs["L0_attn_Bp"], np.float32),
        "dd": np.ascontiguousarray(inputs["L0_attn_D"], np.float32),
        "ls": np.ascontiguousarray(inputs["L0_attn_log_step"], np.float32),
        "gq": np.ascontiguousarray(inputs["L0_ffn_gate_qweight"], np.int32),
        "gs": np.ascontiguousarray(inputs["L0_ffn_gate_scales"], np.float32),
        "gz": np.ascontiguousarray(inputs["L0_ffn_gate_qzeros"], np.int32),
        "uq": np.ascontiguousarray(inputs["L0_ffn_up_qweight"], np.int32),
        "us": np.ascontiguousarray(inputs["L0_ffn_up_scales"], np.float32),
        "uz": np.ascontiguousarray(inputs["L0_ffn_up_qzeros"], np.int32),
        "dq": np.ascontiguousarray(inputs["L0_ffn_down_qweight"], np.int32),
        "ds": np.ascontiguousarray(inputs["L0_ffn_down_scales"], np.float32),
        "dz": np.ascontiguousarray(inputs["L0_ffn_down_qzeros"], np.int32),
    }
    base.update(consts)
    in_maps = []
    for c in range(NCORE):
        m = dict(base)
        m["u"] = np.ascontiguousarray(u_full[T * c:T * (c + 1)])
        in_maps.append(m)
    return in_maps


def kernel(**inputs):
    P = build_program()
    in_maps = make_in_maps(inputs)
    res = run_bass_kernel_spmd(P.nc, in_maps, core_ids=list(range(NCORE)))
    out = np.concatenate([np.asarray(r["out"], np.float32) for r in res.results], axis=0)
    return out.reshape(2, 4096, 4096)
